# version 24
# baseline (speedup 1.0000x reference)
import numpy as np
import ml_dtypes
import contextlib
import concourse.bass as bass
import concourse.mybir as mybir
from concourse.bass_utils import run_bass_kernel_spmd

F32 = mybir.dt.float32
BF16 = mybir.dt.bfloat16
AF = mybir.ActivationFunctionType
ALU = mybir.AluOpType

NCORES = 8
R = 2048
D = 2048
I = 8192
G = 512
NG = R // G
NRT = G // 128
SPAN = 512
NSP = I // SPAN
WSP = 2
NW = NSP // WSP
NCH = I // 128
KT = D // 128

EPS, ONE, C09, CM1, NHALF, PHALF = 0, 1, 2, 3, 4, 5


class Sem:
    def __init__(self, nc, st, name):
        self.h = st.enter_context(nc.semaphore(name))
        self.n = 0


def build_core():
    nc = bass.Bass("TRN2", target_bir_lowering=False)
    x_d = nc.dram_tensor("x", [R, D], F32, kind="ExternalInput").ap()
    gu_d = nc.dram_tensor("gu", [NCH, 128, KT * 256], F32, kind="ExternalInput").ap()
    wd_d = nc.dram_tensor("wd", [NCH, 128, D], F32, kind="ExternalInput").ap()
    w0b_d = nc.dram_tensor("w0b", [128, D], F32, kind="ExternalInput").ap()
    w1b_d = nc.dram_tensor("w1b", [128, D], F32, kind="ExternalInput").ap()
    bb_d = nc.dram_tensor("bb", [128, D], F32, kind="ExternalInput").ap()
    dv_d = nc.dram_tensor("dv", [128, 2], F32, kind="ExternalInput").ap()
    cpk_d = nc.dram_tensor("cpk", [128, 8], F32, kind="ExternalInput").ap()
    eye_d = nc.dram_tensor("eye", [128, 128], BF16, kind="ExternalInput").ap()
    out_d = nc.dram_tensor("out", [R, D], F32, kind="ExternalOutput").ap()
    guc_d = nc.dram_tensor("guc", [NCH, 128, KT * 256], BF16, kind="Internal").ap()
    wdc_d = nc.dram_tensor("wdc", [NCH, 128, D], BF16, kind="Internal").ap()

    with contextlib.ExitStack() as st:
        e = st.enter_context
        A = nc.alloc_sbuf_tensor
        cpk = A("cpk_s", [128, 8], F32).ap()
        w0b = A("w0b_s", [128, D], BF16).ap()
        w1b = A("w1b_s", [128, D], BF16).ap()
        bb = A("bb_s", [128, D], BF16).ap()
        dvb = A("dvb", [128, 2], F32).ap()
        dvc = A("dvc", [128, 2], F32).ap()
        eye = A("eye_s", [128, 128], BF16).ap()
        xt = [A(f"xt{i}", [128, D], F32).ap() for i in range(2)]
        u_ = A("u", [128, D], BF16).ap()
        S0 = A("S0", [128, D], F32).ap()
        S1 = A("S1", [128, D], F32).ap()
        ytmp = A("ytmp", [128, D], F32).ap()
        vTT = [A(f"vTT{i}", [128, KT * G], BF16).ap() for i in range(2)]
        zb = A("zb", [128, NRT * D], F32).ap()
        hsp = A("hsp", [128, 4 * G], F32).ap()
        habs = A("habs", [128, 4 * G], BF16).ap()
        smt = A("smt", [128, 4 * G], BF16).ap()
        t2nd = A("t2nd", [128, G], BF16).ap()
        hs = A("hs", [128, WSP * 4 * G], BF16).ap()
        wpk = [A(f"wpk{i}", [128, KT * 256], BF16).ap() for i in range(3)]
        wdb = A("wdb", [128, WSP * 4 * D], BF16).ap()
        NSET = 2
        st6 = [A(f"st6_{i}", [128, 24], F32).ap() for i in range(NSET)]
        st2 = [A(f"st2_{i}", [128, 2], F32).ap() for i in range(NSET)]
        sdv = [A(f"sdv_{i}", [128, 1], F32).ap() for i in range(NSET)]
        rst = [A(f"rst_{i}", [128, 1], F32).ap() for i in range(NSET)]
        sg_t = A("sg", [128, G], F32).ap()
        pk = [A(f"pk{i}", [128, 8], F32).ap() for i in range(2)]
        vst = S1.bitcast(BF16)

        z_ps = nc.alloc_psum_tensor("z_ps", [128, D], F32).ap()
        gp_ps = [nc.alloc_psum_tensor(f"gp{i}", [128, G], F32).ap()
                 for i in range(2)]
        pp_ps = [nc.alloc_psum_tensor(f"pp{i}", [128, G], F32).ap()
                 for i in range(2)]
        zbT = z_ps.bitcast(BF16)
        trZ = [zbT[:, 0:128], zbT[:, 2048:2176]]
        trF = [gp_ps[0].bitcast(BF16)[:, 0:128], pp_ps[0].bitcast(BF16)[:, 0:128]]

        sxp = [Sem(nc, st, f"sxp{i}") for i in range(2)]
        sw = Sem(nc, st, "sw")
        sc = Sem(nc, st, "scst")
        sop = [Sem(nc, st, f"sop{i}") for i in range(NRT)]
        sv = Sem(nc, st, "sv")
        ss = Sem(nc, st, "ss")
        stt = Sem(nc, st, "stt")
        sgp = Sem(nc, st, "sgp")
        svt = Sem(nc, st, "svt")
        swbg = [Sem(nc, st, f"swbg{i}") for i in range(3)]
        swbw = Sem(nc, st, "swbw")
        swpg = [Sem(nc, st, f"swpg{i}") for i in range(3)]
        swpy = [Sem(nc, st, f"swpy{i}") for i in range(3)]
        sws = Sem(nc, st, "sws")
        scg = Sem(nc, st, "scg")

        streams = {k: [] for k in ("t", "v", "s", "g", "y")}

        def op(eng, builder, waits=(), incs=()):
            streams[eng].append((tuple(waits), builder, tuple(incs)))
            for sem, k in incs:
                sem.n += k

        op("y", lambda q: q.dma_start(out=cpk[:, :], in_=cpk_d[:, :]), incs=[(sc, 16)])
        op("y", lambda q: q.dma_start(out=dvb[:, :], in_=dv_d[:, :]), incs=[(sc, 16)])
        op("y", lambda q: q.dma_start(out=eye[:, :], in_=eye_d[:, :]), incs=[(sc, 16)])
        op("g", lambda q: q.dma_start(out=w0b[:, :], in_=w0b_d[:, :]), incs=[(scg, 16)])
        op("g", lambda q: q.dma_start(out=w1b[:, :], in_=w1b_d[:, :]), incs=[(scg, 16)])
        op("g", lambda q: q.dma_start(out=bb[:, :], in_=bb_d[:, :]), incs=[(scg, 16)])
        consts_ready = 16 * 3
        op("v", lambda q: q.tensor_scalar_max(dvc[:, :], dvb[:, :],
                                              cpk[:, C09:C09 + 1]),
           waits=[(sc, consts_ready)], incs=[(sv, 1)])
        op("v", lambda q: q.tensor_scalar_min(dvc[:, :], dvc[:, :],
                                              cpk[:, ONE:ONE + 1]),
           waits=[(sv, sv.n)], incs=[(sv, 1)])
        dv_ready = sv.n

        ln_eps = cpk[:, EPS:EPS + 1]
        one_c = cpk[:, ONE:ONE + 1]
        nh_c = cpk[:, NHALF:NHALF + 1]
        ph_c = cpk[:, PHALF:PHALF + 1]

        ln_cnt = [0]
        last_vapply = [0] * NSET
        last_ln = [0] * NSET

        def ln_stats(src_ap, extra_waits):
            si = ln_cnt[0] % NSET
            ln_cnt[0] += 1
            for cc in range(4):
                w = list(extra_waits) if cc == 0 else ()
                def b_(q, cc=cc, src=src_ap, si=si):
                    return q.bn_stats(st6[si][:, 6 * cc:6 * cc + 6],
                                      src[:, 512 * cc:512 * cc + 512])
                op("v", b_, waits=w, incs=[(sv, 1)] if cc == 3 else ())
            op("v", lambda q, si=si: q.bn_aggr(st2[si][:, 0:2], st6[si][:, 0:24]),
               waits=[(sv, sv.n), (ss, last_ln[si])], incs=[(sv, 1)])
            vv = sv.n
            op("s", lambda q, si=si: q.activation(sdv[si][:, 0:1], st2[si][:, 1:2],
                                                  AF.Ln, bias=ln_eps, scale=one_c),
               waits=[(sv, vv), (sc, consts_ready)], incs=[(ss, 1)])
            last_ln[si] = ss.n
            op("s", lambda q, si=si: q.activation(rst[si][:, 0:1], sdv[si][:, 0:1],
                                                  AF.Exp, scale=nh_c),
               waits=[(ss, ss.n), (sv, last_vapply[si])], incs=[(ss, 1)])
            return si, ss.n

        def apply_ln(dst_ap, src_ap, si, ssv, extra_waits=()):
            op("v", lambda q, si=si: q.tensor_scalar(
                dst_ap, src_ap, st2[si][:, 0:1], rst[si][:, 0:1],
                ALU.subtract, ALU.mult),
               waits=[(ss, ssv)] + list(extra_waits), incs=[(sv, 1)])
            last_vapply[si] = sv.n
            return sv.n

        x_done = {}
        y_free = {}
        out_sent = {}
        pe_chunk_done = {}
        pe_chunk_mid = {}
        down_done_w = {}
        silu_done_c = {}
        hmul_done_c = {}
        gu_ready = {}
        vt_evict = {}
        vtr_done = [0]
        burst_evict = {}
        last_par0 = [0, 0]
        last_out_add = [0]
        yrec_read = [0]
        last_vrec_tr = [0]
        last_span_mults = [0]
        last_c0h1 = [0]
        wb_done = {}
        wdwb_done = {}
        prev_evict = [0]
        H = D // 2

        def attn_steps(g1, rt, tr_slots, tr_guard_fn):
            gi = g1 * NRT + rt
            par = gi % 2
            row0 = g1 * G + rt * 128
            stv = {}

            def s1():
                w = [(sv, y_free[gi - 2])] if gi - 2 in y_free else []
                op("y", lambda q: q.dma_start(
                    out=xt[par][:, :], in_=x_d[row0:row0 + 128, :]),
                    waits=w, incs=[(sxp[par], 16)])
                x_done[gi] = (sxp[par], sxp[par].n)
                stv['ln1'] = ln_stats(xt[par], [x_done[gi]])

            def s2():
                si, ssv = stv['ln1']
                apply_ln(u_[:, :], xt[par][:, :], si, ssv,
                         extra_waits=[(sgp, last_c0h1[0])]
                         if last_c0h1[0] else ())
                stv['uv'] = sv.n

            def s2b():
                op("g", lambda q: q.tensor_tensor(S0[:, 0:H], u_[:, 0:H],
                                                  w0b[:, 0:H], ALU.mult),
                   waits=[(sv, stv['uv']), (scg, 16 * 3)], incs=[(sgp, 1)])
                stv['c0h0'] = ('g', sgp.n)
                op("g", lambda q: q.tensor_tensor(S0[:, H:D], u_[:, H:D],
                                                  w0b[:, H:D], ALU.mult),
                   waits=[(sgp, sgp.n)], incs=[(sgp, 1)])
                stv['c0h1'] = ('g', sgp.n)
                last_c0h1[0] = sgp.n

            def s3():
                dvf = dvc[:, 1:2].broadcast_to([128, D])
                op("v", lambda q, dvf=dvf: q.tensor_tensor_scan(
                    S1[:, :], dvf, u_[:, :], 0.0, ALU.mult, ALU.add),
                    waits=[(sv, sv.n), (svt, svt.n)], incs=[(sv, 1)])
                stv['s1'] = sv.n

            def s3b():
                op("g", lambda q: q.tensor_tensor(S1[:, :], S1[:, :], w1b[:, :],
                                                  ALU.mult),
                   waits=[(sv, stv['s1']), (scg, 16 * 3)], incs=[(sgp, 1)])
                op("g", lambda q: q.tensor_tensor(S1[:, :], S1[:, :], bb[:, :],
                                                  ALU.add),
                   waits=[(sgp, sgp.n)], incs=[(sgp, 1)])
                stv['t3'] = sgp.n

            def s4():
                dv0 = dvc[:, 0:1].broadcast_to([128, H])
                op("v", lambda q, dv0=dv0: q.tensor_tensor_scan(
                    S0[:, 0:H], dv0, S0[:, 0:H], 0.0, ALU.mult, ALU.add),
                    waits=[(sgp, stv['c0h0'][1])], incs=[(sv, 1)])
                op("v", lambda q, dv0=dv0: q.tensor_tensor_scan(
                    S0[:, H:D], dv0, S0[:, H:D], S0[:, H - 1:H], ALU.mult,
                    ALU.add),
                    waits=[(sgp, stv['c0h1'][1]), (sv, sv.n)], incs=[(sv, 1)])

            def s5():
                op("v", lambda q: q.tensor_tensor(S0[:, :], S0[:, :], S1[:, :],
                                                  ALU.add),
                   waits=[(sv, sv.n), (sgp, stv['t3'])], incs=[(sv, 1)])
                stv['ln2'] = ln_stats(S0, [(sv, sv.n)])

            def s6():
                si, ssv = stv['ln2']
                apply_ln(S1[:, :], S0[:, :], si, ssv)
                op("v", lambda q: q.tensor_tensor(
                    ytmp[:, :], xt[par][:, :], S1[:, :], ALU.add),
                    waits=[(sv, sv.n)], incs=[(sv, 1)])
                y_free[gi] = sv.n
                stv['ln3'] = ln_stats(ytmp, [(sv, sv.n)])

            def s7():
                si, ssv = stv['ln3']
                gpar = g1 % 2
                op("s", lambda q, si=si: q.copy(pk[gpar][:, rt * 2:rt * 2 + 1],
                                                st2[si][:, 0:1]),
                   incs=[(ss, 1)])
                op("s", lambda q, si=si: q.activation(
                    pk[gpar][:, rt * 2 + 1:rt * 2 + 2], sdv[si][:, 0:1],
                    AF.Exp, scale=ph_c),
                   incs=[(ss, 1)])
                last_ln[si] = ss.n
                stv['v'] = apply_ln(vst[:, 0:D], ytmp[:, :], si, ssv)

            def s8():
                dstv = vTT[g1 % 2]
                for k in range(KT):
                    w = []
                    if k == 0:
                        w = [(sv, stv['v']), (stt, last_vrec_tr[0])]
                    op("s", lambda q, k=k, dstv=dstv: q.dma_start(
                        out=dstv[:, k * G + rt * 128:k * G + rt * 128 + 128],
                        in_=vst[:, k * 128:(k + 1) * 128], transpose=True),
                        waits=w, incs=[(svt, 16)])
                vt_evict[(g1, rt)] = svt.n

            return [s1, s2, s2b, s3, s3b, s4, s5, s6, s7, s8]

        def emit_gu(g1, cix_lo, n):
            for cc in range(cix_lo, min(cix_lo + n, NCH)):
                qix = g1 * NCH + cc
                par = qix % 3
                HB = KT * 128
                for h in range(2):
                    w = []
                    if qix >= 3:
                        w.append((stt, pe_chunk_mid[qix - 3] if h == 0
                                  else pe_chunk_done[qix - 3]))
                        if (qix - 3) in wb_done:
                            w.append(wb_done[qix - 3])
                        if g1 >= 1:
                            w.append(wb_done[cc])
                    lo, hi = h * HB, (h + 1) * HB
                    if g1 == 0:
                        op("g", lambda q, cc=cc, par=par, lo=lo, hi=hi:
                           q.dma_start(out=wpk[par][:, lo:hi],
                                       in_=gu_d[cc, :, lo:hi]),
                           waits=w, incs=[(swpg[par], 16)])
                    else:
                        op("y", lambda q, cc=cc, par=par, lo=lo, hi=hi:
                           q.dma_start(out=wpk[par][:, lo:hi],
                                       in_=guc_d[cc, :, lo:hi]),
                           waits=w, incs=[(swpy[par], 16)])
                gu_ready[qix] = (swpg[par], swpg[par].n) if g1 == 0 \
                    else (swpy[par], swpy[par].n)
                if g1 == 0:
                    op("y", lambda q, cc=cc, par=par: q.dma_start(
                        out=guc_d[cc, :, :], in_=wpk[par][:, :]),
                       waits=[gu_ready[qix]], incs=[(swbg[par], 16)])
                    wb_done[qix] = (swbg[par], swbg[par].n)

        def ffn_group(g1, pump, enqueue, run_tr):
            vT_done = max(vt_evict[(g1, rt)] for rt in range(NRT))
            emit_gu(g1, 0, 3)
            for wdx in range(NW):
                enqueue(g1, wdx)
                hs_ready = {}
                for s2 in range(WSP):
                    w = []
                    if s2 == 0:
                        if (g1, wdx - 1) in down_done_w:
                            w.append((stt, down_done_w[(g1, wdx - 1)]))
                        elif wdx == 0 and (g1 - 1, NW - 1) in down_done_w:
                            w.append((stt, down_done_w[(g1 - 1, NW - 1)]))
                        if (g1, wdx - 1) in wdwb_done:
                            w.append(wdwb_done[(g1, wdx - 1)])
                        elif wdx == 0 and (g1 - 1, NW - 1) in wdwb_done:
                            w.append(wdwb_done[(g1 - 1, NW - 1)])
                    c0w = (wdx * WSP + s2) * 4
                    dst = wdb[:, s2 * 4 * D:(s2 + 1) * 4 * D] \
                        .rearrange("p (r h) -> p r h", h=D)
                    if g1 == 0:
                        op("g", lambda q, c0w=c0w, dst=dst: q.dma_start(
                            out=dst, in_=wd_d[c0w:c0w + 4, :, :]
                            .rearrange("r p h -> p r h")),
                            waits=w, incs=[(sw, 16)])
                    else:
                        op("s", lambda q, c0w=c0w, dst=dst: q.dma_start(
                            out=dst, in_=wdc_d[c0w:c0w + 4, :, :]
                            .rearrange("r p h -> p r h")),
                            waits=w, incs=[(sws, 16)])
                wd_ready = (sw, sw.n) if g1 == 0 else (sws, sws.n)
                if g1 == 0:
                    c0w = wdx * WSP * 4
                    op("y", lambda q, c0w=c0w: q.dma_start(
                        out=wdc_d[c0w:c0w + WSP * 4, :, :]
                        .rearrange("r p h -> p r h"),
                        in_=wdb[:, :].rearrange("p (r h) -> p r h", h=D)),
                       waits=[wd_ready], incs=[(swbw, 16)])
                    wdwb_done[(g1, wdx)] = (swbw, swbw.n)
                for s_ in range(WSP):
                    sp = wdx * WSP + s_
                    for r in range(4):
                        cix = sp * 4 + r
                        qix = g1 * NCH + cix
                        par = qix % 3
                        gps, pps = gp_ps[qix % 2], pp_ps[qix % 2]
                        if r == 2 and cix >= 2:
                            emit_gu(g1, cix + 1, 2)
                        w0_ = [gu_ready[qix], (svt, vT_done)]
                        if qix - 2 in silu_done_c:
                            w0_.append((ss, silu_done_c[qix - 2]))
                        if qix - 2 in hmul_done_c:
                            w0_.append((sv, hmul_done_c[qix - 2]))
                        for k in range(KT):
                            op("t", lambda q, par=par, k=k, gps=gps: q.matmul(
                                gps[:, :], wpk[par][:, 256 * k:256 * k + 128],
                                vTT[g1 % 2][:, k * G:(k + 1) * G],
                                start=(k == 0), stop=(k == KT - 1),
                                skip_group_check=True),
                                waits=w0_ if k == 0 else ())
                            op("t", lambda q, par=par, k=k, pps=pps: q.matmul(
                                pps[:, :],
                                wpk[par][:, 256 * k + 128:256 * (k + 1)],
                                vTT[g1 % 2][:, k * G:(k + 1) * G],
                                start=(k == 0), stop=(k == KT - 1),
                                skip_group_check=True),
                                incs=[(stt, 1)]
                                if k in (KT // 2 - 1, KT - 1) else ())
                            if k == KT // 2 - 1:
                                pe_chunk_mid[qix] = stt.n
                        gp_done = stt.n
                        pe_chunk_done[qix] = gp_done
                        sw_ = [(stt, gp_done)]
                        if qix - 1 in hmul_done_c:
                            sw_.append((sv, hmul_done_c[qix - 1]))
                        op("s", lambda q, gps=gps: q.activation(
                            sg_t[:, :], gps[:, :], AF.Silu),
                           waits=sw_, incs=[(ss, 1)])
                        silu_done_c[qix] = ss.n
                        if par == 0:
                            last_par0[0] = ss.n
                        hw = [(ss, silu_done_c[qix])]
                        if r == 0 and last_span_mults[0]:
                            hw.append((sgp, last_span_mults[0]))
                        if qix == g1 * NCH and last_out_add[0]:
                            hw.append((sv, last_out_add[0]))
                        op("v", lambda q, r=r, pps=pps: q.tensor_tensor(
                            hsp[:, r * G:(r + 1) * G], sg_t[:, :], pps[:, :],
                            ALU.mult),
                            waits=hw, incs=[(sv, 1)])
                        hmul_done_c[qix] = sv.n
                        if par == 0:
                            last_par0[1] = sv.n
                        aw = [(sv, hmul_done_c[qix])]
                        if r == 0 and last_span_mults[0]:
                            aw.append((sgp, last_span_mults[0]))
                        if qix == g1 * NCH and last_out_add[0]:
                            aw.append((sv, last_out_add[0]))
                        op("s", lambda q, r=r: q.activation(
                            habs[:, r * G:(r + 1) * G], hsp[:, r * G:(r + 1) * G],
                            AF.Abs),
                           waits=aw, incs=[(ss, 1)])
                        if (sp % WSP) * 4 + r < 6:
                            pump(1)
                    habs_done = ss.n
                    if sp + 1 < NSP:
                        emit_gu(g1, 4 * (sp + 1) + 1, 2)
                    hv = [habs[:, r * G:(r + 1) * G] for r in range(4)]
                    mv = [smt[:, i * G:(i + 1) * G] for i in range(4)]
                    op("v", lambda q, hv=hv, mv=mv: q.tensor_tensor(
                        mv[0], hv[0], hv[1], ALU.max), waits=[(ss, habs_done)])
                    op("v", lambda q, hv=hv, mv=mv: q.tensor_tensor(
                        mv[1], hv[0], hv[1], ALU.min))
                    op("v", lambda q, hv=hv, mv=mv: q.tensor_tensor(
                        mv[2], hv[2], hv[3], ALU.max))
                    op("v", lambda q, hv=hv, mv=mv: q.tensor_tensor(
                        mv[3], hv[2], hv[3], ALU.min), incs=[(sv, 1)])
                    op("v", lambda q, mv=mv: q.tensor_tensor(
                        mv[0], mv[0], mv[2], ALU.min), waits=[(sv, sv.n)])
                    op("v", lambda q, mv=mv: q.tensor_tensor(
                        mv[1], mv[1], mv[3], ALU.max), incs=[(sv, 1)])
                    op("v", lambda q, mv=mv: q.tensor_tensor(
                        t2nd[:, :], mv[0], mv[1], ALU.max),
                       waits=[(sv, sv.n)], incs=[(sv, 1)])
                    for r in range(4):
                        op("v", lambda q, r=r: q.tensor_tensor(
                            habs[:, r * G:(r + 1) * G], habs[:, r * G:(r + 1) * G],
                            t2nd[:, :], ALU.is_ge),
                           waits=[(sv, sv.n)] if r == 0 else (),
                           incs=[(sv, 1)] if r == 3 else ())
                    ge_done = sv.n
                    for r in range(4):
                        slot = s_ * 4 + r
                        w = [(sv, ge_done)] if r == 0 else []
                        if s_ == 0 and r == 0:
                            if (g1, wdx - 1) in down_done_w:
                                w.append((stt, down_done_w[(g1, wdx - 1)]))
                            elif wdx == 0 and (g1 - 1, NW - 1) in down_done_w:
                                w.append((stt, down_done_w[(g1 - 1, NW - 1)]))
                        op("g", lambda q, r=r, slot=slot: q.tensor_tensor(
                            hs[:, slot * G:(slot + 1) * G],
                            hsp[:, r * G:(r + 1) * G],
                            habs[:, r * G:(r + 1) * G], ALU.mult),
                           waits=w, incs=[(sgp, 1)] if r == 3 else ())
                    hs_ready[s_] = sgp.n
                    last_span_mults[0] = sgp.n
                for rt in range(NRT):
                    nt = WSP * 4
                    for tix in range(nt):
                        s_, r = divmod(tix, 4)
                        w = []
                        if r == 0 and rt == 0:
                            w.append((sgp, hs_ready[s_]))
                        if tix == 0:
                            w.append(wd_ready)
                            if prev_evict[0]:
                                w.append((sv, prev_evict[0]))
                        for qq in range(4):
                            op("t", lambda q, tix=tix, rt=rt, qq=qq: q.matmul(
                                z_ps[:, qq * 512:(qq + 1) * 512],
                                hs[:, tix * G + rt * 128:tix * G + rt * 128 + 128],
                                wdb[:, tix * D + qq * 512:tix * D + (qq + 1) * 512],
                                start=(tix == 0), stop=(tix == nt - 1),
                                skip_group_check=True),
                                waits=w if qq == 0 else (),
                                incs=[(stt, 1)]
                                if (qq == 3 and tix == nt - 1) else ())
                        w = []
                    ddone = stt.n
                    if rt == NRT - 1:
                        down_done_w[(g1, wdx)] = ddone
                    if wdx == 0:
                        wz = [(stt, ddone)]
                        if (g1 - 1, rt) in out_sent:
                            wz.append(out_sent[(g1 - 1, rt)])
                        op("v", lambda q, rt=rt: q.tensor_copy(
                            zb[:, rt * D:(rt + 1) * D], z_ps[:, :]),
                           waits=wz, incs=[(sv, 1)])
                    else:
                        op("v", lambda q, rt=rt: q.tensor_tensor(
                            zb[:, rt * D:(rt + 1) * D], zb[:, rt * D:(rt + 1) * D],
                            z_ps[:, :], ALU.add),
                           waits=[(stt, ddone)], incs=[(sv, 1)])
                    prev_evict[0] = sv.n

        def ln4_group(g1, pump):
            z_done = prev_evict[0]
            gpar = g1 % 2
            for rt in range(NRT):
                ev = {}
                for k in range(KT):
                    ps = k % 2
                    w = []
                    if k < 2:
                        w.append((sv, prev_evict[0]))
                        w.append((sgp, sgp.n))
                        if yrec_read[0]:
                            w.append((sv, yrec_read[0]))
                    else:
                        w.append((ss, ev[k - 2]))
                    op("t", lambda q, k=k, ps=ps, g1=g1, rt=rt: q.transpose(
                        trZ[ps], vTT[g1 % 2][:, k * G + rt * 128:
                                             k * G + rt * 128 + 128], eye[:, :]),
                        waits=w, incs=[(stt, 1)])
                    vtr_done[0] = stt.n
                    op("s", lambda q, k=k, ps=ps: q.copy(
                        habs[:, k * 128:(k + 1) * 128], trZ[ps]),
                        waits=[(stt, stt.n)], incs=[(ss, 1)])
                    ev[k] = ss.n
                vrec_done = ss.n
                last_vrec_tr[0] = vtr_done[0]
                si, ssv = ln_stats(zb[:, rt * D:(rt + 1) * D],
                                   [(sv, z_done)] if rt == 0 else [])
                apply_ln(hsp[:, 0:D], zb[:, rt * D:(rt + 1) * D], si, ssv,
                         extra_waits=[(sgp, sgp.n)] if rt == 0 else ())
                op("v", lambda q, rt=rt: q.tensor_scalar(
                    ytmp[:, :], habs[:, 0:D],
                    pk[gpar][:, rt * 2 + 1:rt * 2 + 2],
                    pk[gpar][:, rt * 2:rt * 2 + 1],
                    ALU.mult, ALU.add),
                   waits=[(ss, vrec_done)], incs=[(sv, 1)])
                yrec_read[0] = sv.n
                op("v", lambda q, rt=rt: q.tensor_tensor(
                    zb[:, rt * D:(rt + 1) * D], ytmp[:, :], hsp[:, 0:D],
                    ALU.add),
                   waits=[(sv, sv.n)], incs=[(sv, 1)])
                last_out_add[0] = sv.n
                row0 = g1 * G + rt * 128
                op("y", lambda q, rt=rt, row0=row0: q.dma_start(
                    out=out_d[row0:row0 + 128, :],
                    in_=zb[:, rt * D:(rt + 1) * D]),
                    waits=[(sv, sv.n)], incs=[(sop[rt], 16)])
                out_sent[(g1, rt)] = (sop[rt], sop[rt].n)
                pump(2)

        pending = []
        pend_tr = {}

        def pump(n):
            cnt = 0
            while pending and cnt < n:
                pending.pop(0)()
                cnt += 1

        def enqueue(g1, wdx):
            if g1 + 1 >= NG or wdx % 2 != 0:
                return
            rt = wdx // 2
            sts = attn_steps(g1 + 1, rt, None, None)
            pending.extend(sts[:IL_K])
            tail_steps.append(sts[IL_K:])

        def run_tr(g1, wdx):
            pass

        def no_pump(n):
            pass

        for rt in range(NRT):
            for fn in attn_steps(0, rt, None, None):
                fn()

        def no_enqueue(g1, wdx):
            pass

        def no_tr(g1, wdx):
            pass

        IL_K = 10
        tail_steps = []
        for g in range(NG):
            if g + 1 < NG:
                ffn_group(g, pump, enqueue, run_tr)
            else:
                ffn_group(g, pump, no_enqueue, no_tr)
            pump(99)
            for tail in tail_steps:
                for fn in tail:
                    fn()
            tail_steps = []
            ln4_group(g, no_pump)

        fin = [(s, s.n) for s in sop]
        op("y", lambda q: [q.wait_ge(s.h, v) for s, v in fin])

        block = e(nc.Block())

        def runner(stream):
            def run(q):
                for waits, builder, incs in stream:
                    for sem, val in waits:
                        q.wait_ge(sem.h, val)
                    ins = builder(q)
                    for sem, k in incs:
                        ins.then_inc(sem.h, k)
            return run

        block.tensor(runner(streams["t"]))
        block.vector(runner(streams["v"]))
        block.scalar(runner(streams["s"]))
        block.gpsimd(runner(streams["g"]))
        block.sync(runner(streams["y"]))
    return nc


def kernel(L0_attn_x, L0_attn_weight, L0_attn_bias, L0_attn_decay_value,
           L0_ffn_w_gate, L0_ffn_w_up, L0_ffn_w_down):
    x = np.ascontiguousarray(np.asarray(L0_attn_x, dtype=np.float32)
                             .reshape(NCORES * R, D))
    wg = np.asarray(L0_ffn_w_gate, np.float32).reshape(D, NSP, 128, 4)
    wg = wg.transpose(1, 3, 0, 2).reshape(NCH, D, 128)
    wu = np.asarray(L0_ffn_w_up, np.float32).reshape(D, NSP, 128, 4)
    wu = wu.transpose(1, 3, 0, 2).reshape(NCH, D, 128)
    wdn = np.asarray(L0_ffn_w_down, np.float32).reshape(NSP, 128, 4, D)
    wdn = wdn.transpose(0, 2, 1, 3).reshape(NCH, 128, D)
    wgt = wg.reshape(NCH, KT, 128, 128).transpose(0, 2, 1, 3)
    wut = wu.reshape(NCH, KT, 128, 128).transpose(0, 2, 1, 3)
    gu = np.stack([wgt, wut], axis=3).reshape(NCH, 128, KT * 256)
    gu = np.ascontiguousarray(gu)
    wdn = np.ascontiguousarray(wdn)

    w = np.asarray(L0_attn_weight, np.float32)
    w0b = np.ascontiguousarray(np.broadcast_to(w[0:1, :], (128, D)))
    w1b = np.ascontiguousarray(np.broadcast_to(w[1:2, :], (128, D)))
    bbh = np.ascontiguousarray(np.broadcast_to(
        np.asarray(L0_attn_bias, np.float32)[None, :], (128, D)))
    dv = np.ascontiguousarray(np.broadcast_to(
        np.asarray(L0_attn_decay_value, np.float32).reshape(1, 2), (128, 2)))
    cpk = np.zeros((128, 8), np.float32)
    cpk[:, EPS] = 1e-5
    cpk[:, ONE] = 1.0
    cpk[:, C09] = 0.9
    cpk[:, CM1] = -1.0
    cpk[:, NHALF] = -0.5
    cpk[:, PHALF] = 0.5
    eye = np.eye(128, dtype=ml_dtypes.bfloat16)

    nc = build_core()
    in_maps = []
    for c in range(NCORES):
        in_maps.append({
            "x": x[c * R:(c + 1) * R], "gu": gu, "wd": wdn, "w0b": w0b,
            "w1b": w1b, "bb": bbh, "dv": dv, "cpk": cpk, "eye": eye,
        })
    res = run_bass_kernel_spmd(nc, in_maps, core_ids=list(range(NCORES)))
    out = np.concatenate([r["out"] for r in res.results], axis=0)
    return out.reshape(4, 4096, D)


# version 25
# speedup vs baseline: 1.0322x; 1.0322x over previous
import numpy as np
import ml_dtypes
import contextlib
import concourse.bass as bass
import concourse.mybir as mybir
from concourse.bass_utils import run_bass_kernel_spmd

F32 = mybir.dt.float32
BF16 = mybir.dt.bfloat16
AF = mybir.ActivationFunctionType
ALU = mybir.AluOpType

NCORES = 8
R = 2048
D = 2048
I = 8192
G = 512
NG = R // G
NRT = G // 128
SPAN = 512
NSP = I // SPAN
WSP = 2
NW = NSP // WSP
NCH = I // 128
KT = D // 128

EPS, ONE, C09, CM1, NHALF, PHALF = 0, 1, 2, 3, 4, 5


class Sem:
    def __init__(self, nc, st, name):
        self.h = st.enter_context(nc.semaphore(name))
        self.n = 0


def build_core():
    nc = bass.Bass("TRN2", target_bir_lowering=False)
    x_d = nc.dram_tensor("x", [R, D], F32, kind="ExternalInput").ap()
    gu_d = nc.dram_tensor("gu", [NCH, 128, KT * 256], F32, kind="ExternalInput").ap()
    wd_d = nc.dram_tensor("wd", [NCH, 128, D], F32, kind="ExternalInput").ap()
    w0b_d = nc.dram_tensor("w0b", [128, D], F32, kind="ExternalInput").ap()
    w1b_d = nc.dram_tensor("w1b", [128, D], F32, kind="ExternalInput").ap()
    bb_d = nc.dram_tensor("bb", [128, D], F32, kind="ExternalInput").ap()
    dv_d = nc.dram_tensor("dv", [128, 2], F32, kind="ExternalInput").ap()
    cpk_d = nc.dram_tensor("cpk", [128, 8], F32, kind="ExternalInput").ap()
    eye_d = nc.dram_tensor("eye", [128, 128], BF16, kind="ExternalInput").ap()
    out_d = nc.dram_tensor("out", [R, D], F32, kind="ExternalOutput").ap()
    guc_d = nc.dram_tensor("guc", [NCH, 128, KT * 256], BF16, kind="Internal").ap()
    wdc_d = nc.dram_tensor("wdc", [NCH, 128, D], BF16, kind="Internal").ap()

    with contextlib.ExitStack() as st:
        e = st.enter_context
        A = nc.alloc_sbuf_tensor
        cpk = A("cpk_s", [128, 8], F32).ap()
        w0b = A("w0b_s", [128, D], BF16).ap()
        w1b = A("w1b_s", [128, D], BF16).ap()
        bb = A("bb_s", [128, D], BF16).ap()
        dvb = A("dvb", [128, 2], F32).ap()
        dvc = A("dvc", [128, 2], F32).ap()
        eye = A("eye_s", [128, 128], BF16).ap()
        xt = [A(f"xt{i}", [128, D], F32).ap() for i in range(2)]
        u_ = A("u", [128, D], BF16).ap()
        S0 = A("S0", [128, D], F32).ap()
        S1 = A("S1", [128, D], F32).ap()
        ytmp = A("ytmp", [128, D], F32).ap()
        vTT = [A(f"vTT{i}", [128, KT * G], BF16).ap() for i in range(2)]
        zb = A("zb", [128, NRT * D], F32).ap()
        hsp = A("hsp", [128, 4 * G], F32).ap()
        habs = A("habs", [128, 4 * G], BF16).ap()
        smt = A("smt", [128, 4 * G], BF16).ap()
        t2nd = A("t2nd", [128, G], BF16).ap()
        hs = A("hs", [128, WSP * 4 * G], BF16).ap()
        wpk = [A(f"wpk{i}", [128, KT * 256], BF16).ap() for i in range(3)]
        wdb = A("wdb", [128, WSP * 4 * D], BF16).ap()
        NSET = 2
        st6 = [A(f"st6_{i}", [128, 24], F32).ap() for i in range(NSET)]
        st2 = [A(f"st2_{i}", [128, 2], F32).ap() for i in range(NSET)]
        sdv = [A(f"sdv_{i}", [128, 1], F32).ap() for i in range(NSET)]
        rst = [A(f"rst_{i}", [128, 1], F32).ap() for i in range(NSET)]
        sg_t = A("sg", [128, G], F32).ap()
        pk = [A(f"pk{i}", [128, 8], F32).ap() for i in range(2)]
        vst = S1.bitcast(BF16)

        z_ps = nc.alloc_psum_tensor("z_ps", [128, D], F32).ap()
        gp_ps = [nc.alloc_psum_tensor(f"gp{i}", [128, G], F32).ap()
                 for i in range(2)]
        pp_ps = [nc.alloc_psum_tensor(f"pp{i}", [128, G], F32).ap()
                 for i in range(2)]
        zbT = z_ps.bitcast(BF16)
        trZ = [zbT[:, 0:128], zbT[:, 2048:2176]]
        trF = [gp_ps[0].bitcast(BF16)[:, 0:128], pp_ps[0].bitcast(BF16)[:, 0:128]]

        sxp = [Sem(nc, st, f"sxp{i}") for i in range(2)]
        sw = Sem(nc, st, "sw")
        sc = Sem(nc, st, "scst")
        sop = [Sem(nc, st, f"sop{i}") for i in range(NRT)]
        sv = Sem(nc, st, "sv")
        ss = Sem(nc, st, "ss")
        stt = Sem(nc, st, "stt")
        sgp = Sem(nc, st, "sgp")
        svt = Sem(nc, st, "svt")
        swbg = [Sem(nc, st, f"swbg{i}") for i in range(3)]
        swbw = Sem(nc, st, "swbw")
        swpg = [Sem(nc, st, f"swpg{i}") for i in range(3)]
        swpy = [Sem(nc, st, f"swpy{i}") for i in range(3)]
        sws = Sem(nc, st, "sws")
        scg = Sem(nc, st, "scg")

        streams = {k: [] for k in ("t", "v", "s", "g", "y")}

        def op(eng, builder, waits=(), incs=()):
            streams[eng].append((tuple(waits), builder, tuple(incs)))
            for sem, k in incs:
                sem.n += k

        op("y", lambda q: q.dma_start(out=cpk[:, :], in_=cpk_d[:, :]), incs=[(sc, 16)])
        op("y", lambda q: q.dma_start(out=dvb[:, :], in_=dv_d[:, :]), incs=[(sc, 16)])
        op("y", lambda q: q.dma_start(out=eye[:, :], in_=eye_d[:, :]), incs=[(sc, 16)])
        op("g", lambda q: q.dma_start(out=w0b[:, :], in_=w0b_d[:, :]), incs=[(scg, 16)])
        op("g", lambda q: q.dma_start(out=w1b[:, :], in_=w1b_d[:, :]), incs=[(scg, 16)])
        op("g", lambda q: q.dma_start(out=bb[:, :], in_=bb_d[:, :]), incs=[(scg, 16)])
        consts_ready = 16 * 3
        op("v", lambda q: q.tensor_scalar_max(dvc[:, :], dvb[:, :],
                                              cpk[:, C09:C09 + 1]),
           waits=[(sc, consts_ready)], incs=[(sv, 1)])
        op("v", lambda q: q.tensor_scalar_min(dvc[:, :], dvc[:, :],
                                              cpk[:, ONE:ONE + 1]),
           waits=[(sv, sv.n)], incs=[(sv, 1)])
        dv_ready = sv.n

        ln_eps = cpk[:, EPS:EPS + 1]
        one_c = cpk[:, ONE:ONE + 1]
        nh_c = cpk[:, NHALF:NHALF + 1]
        ph_c = cpk[:, PHALF:PHALF + 1]

        ln_cnt = [0]
        last_vapply = [0] * NSET
        last_ln = [0] * NSET

        def ln_stats(src_ap, extra_waits):
            si = ln_cnt[0] % NSET
            ln_cnt[0] += 1
            for cc in range(4):
                w = list(extra_waits) if cc == 0 else ()
                def b_(q, cc=cc, src=src_ap, si=si):
                    return q.bn_stats(st6[si][:, 6 * cc:6 * cc + 6],
                                      src[:, 512 * cc:512 * cc + 512])
                op("v", b_, waits=w, incs=[(sv, 1)] if cc == 3 else ())
            op("v", lambda q, si=si: q.bn_aggr(st2[si][:, 0:2], st6[si][:, 0:24]),
               waits=[(sv, sv.n), (ss, last_ln[si])], incs=[(sv, 1)])
            vv = sv.n
            op("s", lambda q, si=si: q.activation(sdv[si][:, 0:1], st2[si][:, 1:2],
                                                  AF.Ln, bias=ln_eps, scale=one_c),
               waits=[(sv, vv), (sc, consts_ready)], incs=[(ss, 1)])
            last_ln[si] = ss.n
            op("s", lambda q, si=si: q.activation(rst[si][:, 0:1], sdv[si][:, 0:1],
                                                  AF.Exp, scale=nh_c),
               waits=[(ss, ss.n), (sv, last_vapply[si])], incs=[(ss, 1)])
            return si, ss.n

        def apply_ln(dst_ap, src_ap, si, ssv, extra_waits=()):
            op("v", lambda q, si=si: q.tensor_scalar(
                dst_ap, src_ap, st2[si][:, 0:1], rst[si][:, 0:1],
                ALU.subtract, ALU.mult),
               waits=[(ss, ssv)] + list(extra_waits), incs=[(sv, 1)])
            last_vapply[si] = sv.n
            return sv.n

        x_done = {}
        y_free = {}
        out_sent = {}
        pe_chunk_done = {}
        pe_chunk_mid = {}
        down_done_w = {}
        silu_done_c = {}
        hmul_done_c = {}
        gu_ready = {}
        vt_evict = {}
        vtr_done = [0]
        burst_evict = {}
        last_par0 = [0, 0]
        last_out_add = [0]
        yrec_read = [0]
        last_vrec_tr = [0]
        last_span_mults = [0]
        last_c0h1 = [0]
        wb_done = {}
        wdwb_done = {}
        prev_evict = [0]
        H = D // 2

        def attn_steps(g1, rt, tr_slots, tr_guard_fn):
            gi = g1 * NRT + rt
            par = gi % 2
            row0 = g1 * G + rt * 128
            stv = {}

            def s1():
                w = [(sv, y_free[gi - 2])] if gi - 2 in y_free else []
                op("y", lambda q: q.dma_start(
                    out=xt[par][:, :], in_=x_d[row0:row0 + 128, :]),
                    waits=w, incs=[(sxp[par], 16)])
                x_done[gi] = (sxp[par], sxp[par].n)
                stv['ln1'] = ln_stats(xt[par], [x_done[gi]])

            def s2():
                si, ssv = stv['ln1']
                apply_ln(u_[:, :], xt[par][:, :], si, ssv)
                stv['uv'] = sv.n

            def s2b():
                op("v", lambda q: q.tensor_tensor(S0[:, 0:H], u_[:, 0:H],
                                                  w0b[:, 0:H], ALU.mult),
                   waits=[(sv, sv.n), (scg, 16 * 3)], incs=[(sv, 1)])
                stv['c0h0'] = sv.n
                op("v", lambda q: q.tensor_tensor(S0[:, H:D], u_[:, H:D],
                                                  w0b[:, H:D], ALU.mult),
                   waits=[(sv, sv.n)], incs=[(sv, 1)])
                stv['c0h1'] = sv.n

            def s3():
                dvf = dvc[:, 1:2].broadcast_to([128, D])
                op("v", lambda q, dvf=dvf: q.tensor_tensor_scan(
                    S1[:, :], dvf, u_[:, :], 0.0, ALU.mult, ALU.add),
                    waits=[(sv, sv.n), (svt, svt.n)], incs=[(sv, 1)])
                stv['s1'] = sv.n

            def s3b():
                op("v", lambda q: q.tensor_tensor(S1[:, :], S1[:, :], w1b[:, :],
                                                  ALU.mult),
                   waits=[(sv, sv.n), (scg, 16 * 3)], incs=[(sv, 1)])
                op("v", lambda q: q.tensor_tensor(S1[:, :], S1[:, :], bb[:, :],
                                                  ALU.add),
                   waits=[(sv, sv.n)], incs=[(sv, 1)])
                stv['t3'] = sv.n

            def s4():
                dv0 = dvc[:, 0:1].broadcast_to([128, H])
                op("v", lambda q, dv0=dv0: q.tensor_tensor_scan(
                    S0[:, 0:H], dv0, S0[:, 0:H], 0.0, ALU.mult, ALU.add),
                    waits=[(sv, stv['c0h0'])], incs=[(sv, 1)])
                op("v", lambda q, dv0=dv0: q.tensor_tensor_scan(
                    S0[:, H:D], dv0, S0[:, H:D], S0[:, H - 1:H], ALU.mult,
                    ALU.add),
                    waits=[(sv, sv.n)], incs=[(sv, 1)])

            def s5():
                op("v", lambda q: q.tensor_tensor(S0[:, :], S0[:, :], S1[:, :],
                                                  ALU.add),
                   waits=[(sv, sv.n)], incs=[(sv, 1)])
                stv['ln2'] = ln_stats(S0, [(sv, sv.n)])

            def s6():
                si, ssv = stv['ln2']
                apply_ln(S1[:, :], S0[:, :], si, ssv)
                op("v", lambda q: q.tensor_tensor(
                    ytmp[:, :], xt[par][:, :], S1[:, :], ALU.add),
                    waits=[(sv, sv.n)], incs=[(sv, 1)])
                y_free[gi] = sv.n
                stv['ln3'] = ln_stats(ytmp, [(sv, sv.n)])

            def s7():
                si, ssv = stv['ln3']
                gpar = g1 % 2
                op("s", lambda q, si=si: q.copy(pk[gpar][:, rt * 2:rt * 2 + 1],
                                                st2[si][:, 0:1]),
                   incs=[(ss, 1)])
                op("s", lambda q, si=si: q.activation(
                    pk[gpar][:, rt * 2 + 1:rt * 2 + 2], sdv[si][:, 0:1],
                    AF.Exp, scale=ph_c),
                   incs=[(ss, 1)])
                last_ln[si] = ss.n
                stv['v'] = apply_ln(vst[:, 0:D], ytmp[:, :], si, ssv)

            def s8():
                dstv = vTT[g1 % 2]
                for k in range(KT):
                    w = []
                    if k == 0:
                        w = [(sv, stv['v']), (stt, last_vrec_tr[0])]
                    op("s", lambda q, k=k, dstv=dstv: q.dma_start(
                        out=dstv[:, k * G + rt * 128:k * G + rt * 128 + 128],
                        in_=vst[:, k * 128:(k + 1) * 128], transpose=True),
                        waits=w, incs=[(svt, 16)])
                vt_evict[(g1, rt)] = svt.n

            return [s1, s2, s2b, s3, s3b, s4, s5, s6, s7, s8]

        def emit_gu(g1, cix_lo, n):
            for cc in range(cix_lo, min(cix_lo + n, NCH)):
                qix = g1 * NCH + cc
                par = qix % 3
                HB = KT * 128
                for h in range(2):
                    w = []
                    if qix >= 3:
                        w.append((stt, pe_chunk_mid[qix - 3] if h == 0
                                  else pe_chunk_done[qix - 3]))
                        if (qix - 3) in wb_done:
                            w.append(wb_done[qix - 3])
                        if g1 >= 1:
                            w.append(wb_done[cc])
                    lo, hi = h * HB, (h + 1) * HB
                    if g1 == 0:
                        op("g", lambda q, cc=cc, par=par, lo=lo, hi=hi:
                           q.dma_start(out=wpk[par][:, lo:hi],
                                       in_=gu_d[cc, :, lo:hi]),
                           waits=w, incs=[(swpg[par], 16)])
                    else:
                        op("y", lambda q, cc=cc, par=par, lo=lo, hi=hi:
                           q.dma_start(out=wpk[par][:, lo:hi],
                                       in_=guc_d[cc, :, lo:hi]),
                           waits=w, incs=[(swpy[par], 16)])
                gu_ready[qix] = (swpg[par], swpg[par].n) if g1 == 0 \
                    else (swpy[par], swpy[par].n)
                if g1 == 0:
                    op("y", lambda q, cc=cc, par=par: q.dma_start(
                        out=guc_d[cc, :, :], in_=wpk[par][:, :]),
                       waits=[gu_ready[qix]], incs=[(swbg[par], 16)])
                    wb_done[qix] = (swbg[par], swbg[par].n)

        def ffn_group(g1, pump, enqueue, run_tr):
            vT_done = max(vt_evict[(g1, rt)] for rt in range(NRT))
            emit_gu(g1, 0, 3)
            for wdx in range(NW):
                enqueue(g1, wdx)
                hs_ready = {}
                for s2 in range(WSP):
                    w = []
                    if s2 == 0:
                        if (g1, wdx - 1) in down_done_w:
                            w.append((stt, down_done_w[(g1, wdx - 1)]))
                        elif wdx == 0 and (g1 - 1, NW - 1) in down_done_w:
                            w.append((stt, down_done_w[(g1 - 1, NW - 1)]))
                        if (g1, wdx - 1) in wdwb_done:
                            w.append(wdwb_done[(g1, wdx - 1)])
                        elif wdx == 0 and (g1 - 1, NW - 1) in wdwb_done:
                            w.append(wdwb_done[(g1 - 1, NW - 1)])
                    c0w = (wdx * WSP + s2) * 4
                    dst = wdb[:, s2 * 4 * D:(s2 + 1) * 4 * D] \
                        .rearrange("p (r h) -> p r h", h=D)
                    if g1 == 0:
                        op("g", lambda q, c0w=c0w, dst=dst: q.dma_start(
                            out=dst, in_=wd_d[c0w:c0w + 4, :, :]
                            .rearrange("r p h -> p r h")),
                            waits=w, incs=[(sw, 16)])
                    else:
                        op("s", lambda q, c0w=c0w, dst=dst: q.dma_start(
                            out=dst, in_=wdc_d[c0w:c0w + 4, :, :]
                            .rearrange("r p h -> p r h")),
                            waits=w, incs=[(sws, 16)])
                wd_ready = (sw, sw.n) if g1 == 0 else (sws, sws.n)
                if g1 == 0:
                    c0w = wdx * WSP * 4
                    op("y", lambda q, c0w=c0w: q.dma_start(
                        out=wdc_d[c0w:c0w + WSP * 4, :, :]
                        .rearrange("r p h -> p r h"),
                        in_=wdb[:, :].rearrange("p (r h) -> p r h", h=D)),
                       waits=[wd_ready], incs=[(swbw, 16)])
                    wdwb_done[(g1, wdx)] = (swbw, swbw.n)
                for s_ in range(WSP):
                    sp = wdx * WSP + s_
                    for r in range(4):
                        cix = sp * 4 + r
                        qix = g1 * NCH + cix
                        par = qix % 3
                        gps, pps = gp_ps[qix % 2], pp_ps[qix % 2]
                        if r == 2 and cix >= 2:
                            emit_gu(g1, cix + 1, 2)
                        w0_ = [gu_ready[qix], (svt, vT_done)]
                        if qix - 2 in silu_done_c:
                            w0_.append((ss, silu_done_c[qix - 2]))
                        if qix - 2 in hmul_done_c:
                            w0_.append((sv, hmul_done_c[qix - 2]))
                        for k in range(KT):
                            op("t", lambda q, par=par, k=k, gps=gps: q.matmul(
                                gps[:, :], wpk[par][:, 256 * k:256 * k + 128],
                                vTT[g1 % 2][:, k * G:(k + 1) * G],
                                start=(k == 0), stop=(k == KT - 1),
                                skip_group_check=True),
                                waits=w0_ if k == 0 else ())
                            op("t", lambda q, par=par, k=k, pps=pps: q.matmul(
                                pps[:, :],
                                wpk[par][:, 256 * k + 128:256 * (k + 1)],
                                vTT[g1 % 2][:, k * G:(k + 1) * G],
                                start=(k == 0), stop=(k == KT - 1),
                                skip_group_check=True),
                                incs=[(stt, 1)]
                                if k in (KT // 2 - 1, KT - 1) else ())
                            if k == KT // 2 - 1:
                                pe_chunk_mid[qix] = stt.n
                        gp_done = stt.n
                        pe_chunk_done[qix] = gp_done
                        sw_ = [(stt, gp_done)]
                        if qix - 1 in hmul_done_c:
                            sw_.append((sv, hmul_done_c[qix - 1]))
                        op("s", lambda q, gps=gps: q.activation(
                            sg_t[:, :], gps[:, :], AF.Silu),
                           waits=sw_, incs=[(ss, 1)])
                        silu_done_c[qix] = ss.n
                        if par == 0:
                            last_par0[0] = ss.n
                        hw = [(ss, silu_done_c[qix])]
                        if r == 0 and last_span_mults[0]:
                            hw.append((sgp, last_span_mults[0]))
                        if qix == g1 * NCH and last_out_add[0]:
                            hw.append((sv, last_out_add[0]))
                        op("v", lambda q, r=r, pps=pps: q.tensor_tensor(
                            hsp[:, r * G:(r + 1) * G], sg_t[:, :], pps[:, :],
                            ALU.mult),
                            waits=hw, incs=[(sv, 1)])
                        hmul_done_c[qix] = sv.n
                        if par == 0:
                            last_par0[1] = sv.n
                        aw = [(sv, hmul_done_c[qix])]
                        if r == 0 and last_span_mults[0]:
                            aw.append((sgp, last_span_mults[0]))
                        if qix == g1 * NCH and last_out_add[0]:
                            aw.append((sv, last_out_add[0]))
                        op("s", lambda q, r=r: q.activation(
                            habs[:, r * G:(r + 1) * G], hsp[:, r * G:(r + 1) * G],
                            AF.Abs),
                           waits=aw, incs=[(ss, 1)])
                        if (sp % WSP) * 4 + r < 6:
                            pump(1)
                    habs_done = ss.n
                    if sp + 1 < NSP:
                        emit_gu(g1, 4 * (sp + 1) + 1, 2)
                    hv = [habs[:, r * G:(r + 1) * G] for r in range(4)]
                    mv = [smt[:, i * G:(i + 1) * G] for i in range(4)]
                    op("v", lambda q, hv=hv, mv=mv: q.tensor_tensor(
                        mv[0], hv[0], hv[1], ALU.max), waits=[(ss, habs_done)])
                    op("v", lambda q, hv=hv, mv=mv: q.tensor_tensor(
                        mv[1], hv[0], hv[1], ALU.min))
                    op("v", lambda q, hv=hv, mv=mv: q.tensor_tensor(
                        mv[2], hv[2], hv[3], ALU.max))
                    op("v", lambda q, hv=hv, mv=mv: q.tensor_tensor(
                        mv[3], hv[2], hv[3], ALU.min), incs=[(sv, 1)])
                    op("v", lambda q, mv=mv: q.tensor_tensor(
                        mv[0], mv[0], mv[2], ALU.min), waits=[(sv, sv.n)])
                    op("v", lambda q, mv=mv: q.tensor_tensor(
                        mv[1], mv[1], mv[3], ALU.max), incs=[(sv, 1)])
                    op("v", lambda q, mv=mv: q.tensor_tensor(
                        t2nd[:, :], mv[0], mv[1], ALU.max),
                       waits=[(sv, sv.n)], incs=[(sv, 1)])
                    for r in range(4):
                        op("v", lambda q, r=r: q.tensor_tensor(
                            habs[:, r * G:(r + 1) * G], habs[:, r * G:(r + 1) * G],
                            t2nd[:, :], ALU.is_ge),
                           waits=[(sv, sv.n)] if r == 0 else (),
                           incs=[(sv, 1)] if r == 3 else ())
                    ge_done = sv.n
                    for r in range(4):
                        slot = s_ * 4 + r
                        w = [(sv, ge_done)] if r == 0 else []
                        if s_ == 0 and r == 0:
                            if (g1, wdx - 1) in down_done_w:
                                w.append((stt, down_done_w[(g1, wdx - 1)]))
                            elif wdx == 0 and (g1 - 1, NW - 1) in down_done_w:
                                w.append((stt, down_done_w[(g1 - 1, NW - 1)]))
                        op("g", lambda q, r=r, slot=slot: q.tensor_tensor(
                            hs[:, slot * G:(slot + 1) * G],
                            hsp[:, r * G:(r + 1) * G],
                            habs[:, r * G:(r + 1) * G], ALU.mult),
                           waits=w, incs=[(sgp, 1)] if r == 3 else ())
                    hs_ready[s_] = sgp.n
                    last_span_mults[0] = sgp.n
                for rt in range(NRT):
                    nt = WSP * 4
                    for tix in range(nt):
                        s_, r = divmod(tix, 4)
                        w = []
                        if r == 0 and rt == 0:
                            w.append((sgp, hs_ready[s_]))
                        if tix == 0:
                            w.append(wd_ready)
                            if prev_evict[0]:
                                w.append((sv, prev_evict[0]))
                        for qq in range(4):
                            op("t", lambda q, tix=tix, rt=rt, qq=qq: q.matmul(
                                z_ps[:, qq * 512:(qq + 1) * 512],
                                hs[:, tix * G + rt * 128:tix * G + rt * 128 + 128],
                                wdb[:, tix * D + qq * 512:tix * D + (qq + 1) * 512],
                                start=(tix == 0), stop=(tix == nt - 1),
                                skip_group_check=True),
                                waits=w if qq == 0 else (),
                                incs=[(stt, 1)]
                                if (qq == 3 and tix == nt - 1) else ())
                        w = []
                    ddone = stt.n
                    if rt == NRT - 1:
                        down_done_w[(g1, wdx)] = ddone
                    if wdx == 0:
                        wz = [(stt, ddone)]
                        if (g1 - 1, rt) in out_sent:
                            wz.append(out_sent[(g1 - 1, rt)])
                        op("v", lambda q, rt=rt: q.tensor_copy(
                            zb[:, rt * D:(rt + 1) * D], z_ps[:, :]),
                           waits=wz, incs=[(sv, 1)])
                    else:
                        op("v", lambda q, rt=rt: q.tensor_tensor(
                            zb[:, rt * D:(rt + 1) * D], zb[:, rt * D:(rt + 1) * D],
                            z_ps[:, :], ALU.add),
                           waits=[(stt, ddone)], incs=[(sv, 1)])
                    prev_evict[0] = sv.n

        def ln4_group(g1, pump):
            z_done = prev_evict[0]
            gpar = g1 % 2
            for rt in range(NRT):
                ev = {}
                for k in range(KT):
                    ps = k % 2
                    w = []
                    if k < 2:
                        w.append((sv, prev_evict[0]))
                        w.append((sgp, sgp.n))
                        if yrec_read[0]:
                            w.append((sv, yrec_read[0]))
                    else:
                        w.append((ss, ev[k - 2]))
                    op("t", lambda q, k=k, ps=ps, g1=g1, rt=rt: q.transpose(
                        trZ[ps], vTT[g1 % 2][:, k * G + rt * 128:
                                             k * G + rt * 128 + 128], eye[:, :]),
                        waits=w, incs=[(stt, 1)])
                    vtr_done[0] = stt.n
                    op("s", lambda q, k=k, ps=ps: q.copy(
                        habs[:, k * 128:(k + 1) * 128], trZ[ps]),
                        waits=[(stt, stt.n)], incs=[(ss, 1)])
                    ev[k] = ss.n
                vrec_done = ss.n
                last_vrec_tr[0] = vtr_done[0]
                si, ssv = ln_stats(zb[:, rt * D:(rt + 1) * D],
                                   [(sv, z_done)] if rt == 0 else [])
                apply_ln(hsp[:, 0:D], zb[:, rt * D:(rt + 1) * D], si, ssv,
                         extra_waits=[(sgp, sgp.n)] if rt == 0 else ())
                op("v", lambda q, rt=rt: q.tensor_scalar(
                    ytmp[:, :], habs[:, 0:D],
                    pk[gpar][:, rt * 2 + 1:rt * 2 + 2],
                    pk[gpar][:, rt * 2:rt * 2 + 1],
                    ALU.mult, ALU.add),
                   waits=[(ss, vrec_done)], incs=[(sv, 1)])
                yrec_read[0] = sv.n
                op("v", lambda q, rt=rt: q.tensor_tensor(
                    zb[:, rt * D:(rt + 1) * D], ytmp[:, :], hsp[:, 0:D],
                    ALU.add),
                   waits=[(sv, sv.n)], incs=[(sv, 1)])
                last_out_add[0] = sv.n
                row0 = g1 * G + rt * 128
                op("y", lambda q, rt=rt, row0=row0: q.dma_start(
                    out=out_d[row0:row0 + 128, :],
                    in_=zb[:, rt * D:(rt + 1) * D]),
                    waits=[(sv, sv.n)], incs=[(sop[rt], 16)])
                out_sent[(g1, rt)] = (sop[rt], sop[rt].n)
                pump(2)

        pending = []
        pend_tr = {}

        def pump(n):
            cnt = 0
            while pending and cnt < n:
                pending.pop(0)()
                cnt += 1

        def enqueue(g1, wdx):
            if g1 + 1 >= NG or wdx % 2 != 0:
                return
            rt = wdx // 2
            sts = attn_steps(g1 + 1, rt, None, None)
            pending.extend(sts[:IL_K])
            tail_steps.append(sts[IL_K:])

        def run_tr(g1, wdx):
            pass

        def no_pump(n):
            pass

        for rt in range(NRT):
            for fn in attn_steps(0, rt, None, None):
                fn()

        def no_enqueue(g1, wdx):
            pass

        def no_tr(g1, wdx):
            pass

        IL_K = 10
        tail_steps = []
        for g in range(NG):
            if g + 1 < NG:
                ffn_group(g, pump, enqueue, run_tr)
            else:
                ffn_group(g, pump, no_enqueue, no_tr)
            pump(99)
            for tail in tail_steps:
                for fn in tail:
                    fn()
            tail_steps = []
            ln4_group(g, no_pump)

        fin = [(s, s.n) for s in sop]
        op("y", lambda q: [q.wait_ge(s.h, v) for s, v in fin])

        block = e(nc.Block())

        def runner(stream):
            def run(q):
                for waits, builder, incs in stream:
                    for sem, val in waits:
                        q.wait_ge(sem.h, val)
                    ins = builder(q)
                    for sem, k in incs:
                        ins.then_inc(sem.h, k)
            return run

        block.tensor(runner(streams["t"]))
        block.vector(runner(streams["v"]))
        block.scalar(runner(streams["s"]))
        block.gpsimd(runner(streams["g"]))
        block.sync(runner(streams["y"]))
    return nc


def kernel(L0_attn_x, L0_attn_weight, L0_attn_bias, L0_attn_decay_value,
           L0_ffn_w_gate, L0_ffn_w_up, L0_ffn_w_down):
    x = np.ascontiguousarray(np.asarray(L0_attn_x, dtype=np.float32)
                             .reshape(NCORES * R, D))
    wg = np.asarray(L0_ffn_w_gate, np.float32).reshape(D, NSP, 128, 4)
    wg = wg.transpose(1, 3, 0, 2).reshape(NCH, D, 128)
    wu = np.asarray(L0_ffn_w_up, np.float32).reshape(D, NSP, 128, 4)
    wu = wu.transpose(1, 3, 0, 2).reshape(NCH, D, 128)
    wdn = np.asarray(L0_ffn_w_down, np.float32).reshape(NSP, 128, 4, D)
    wdn = wdn.transpose(0, 2, 1, 3).reshape(NCH, 128, D)
    wgt = wg.reshape(NCH, KT, 128, 128).transpose(0, 2, 1, 3)
    wut = wu.reshape(NCH, KT, 128, 128).transpose(0, 2, 1, 3)
    gu = np.stack([wgt, wut], axis=3).reshape(NCH, 128, KT * 256)
    gu = np.ascontiguousarray(gu)
    wdn = np.ascontiguousarray(wdn)

    w = np.asarray(L0_attn_weight, np.float32)
    w0b = np.ascontiguousarray(np.broadcast_to(w[0:1, :], (128, D)))
    w1b = np.ascontiguousarray(np.broadcast_to(w[1:2, :], (128, D)))
    bbh = np.ascontiguousarray(np.broadcast_to(
        np.asarray(L0_attn_bias, np.float32)[None, :], (128, D)))
    dv = np.ascontiguousarray(np.broadcast_to(
        np.asarray(L0_attn_decay_value, np.float32).reshape(1, 2), (128, 2)))
    cpk = np.zeros((128, 8), np.float32)
    cpk[:, EPS] = 1e-5
    cpk[:, ONE] = 1.0
    cpk[:, C09] = 0.9
    cpk[:, CM1] = -1.0
    cpk[:, NHALF] = -0.5
    cpk[:, PHALF] = 0.5
    eye = np.eye(128, dtype=ml_dtypes.bfloat16)

    nc = build_core()
    in_maps = []
    for c in range(NCORES):
        in_maps.append({
            "x": x[c * R:(c + 1) * R], "gu": gu, "wd": wdn, "w0b": w0b,
            "w1b": w1b, "bb": bbh, "dv": dv, "cpk": cpk, "eye": eye,
        })
    res = run_bass_kernel_spmd(nc, in_maps, core_ids=list(range(NCORES)))
    out = np.concatenate([r["out"] for r in res.results], axis=0)
    return out.reshape(4, 4096, D)


# version 26
# speedup vs baseline: 1.0378x; 1.0055x over previous
import numpy as np
import ml_dtypes
import contextlib
import concourse.bass as bass
import concourse.mybir as mybir
from concourse.bass_utils import run_bass_kernel_spmd

F32 = mybir.dt.float32
BF16 = mybir.dt.bfloat16
AF = mybir.ActivationFunctionType
ALU = mybir.AluOpType

NCORES = 8
R = 2048
D = 2048
I = 8192
G = 512
NG = R // G
NRT = G // 128
SPAN = 512
NSP = I // SPAN
WSP = 2
NW = NSP // WSP
NCH = I // 128
KT = D // 128

EPS, ONE, C09, CM1, NHALF, PHALF = 0, 1, 2, 3, 4, 5


class Sem:
    def __init__(self, nc, st, name):
        self.h = st.enter_context(nc.semaphore(name))
        self.n = 0


def build_core():
    nc = bass.Bass("TRN2", target_bir_lowering=False)
    x_d = nc.dram_tensor("x", [R, D], F32, kind="ExternalInput").ap()
    gu_d = nc.dram_tensor("gu", [NCH, 128, KT * 256], F32, kind="ExternalInput").ap()
    wd_d = nc.dram_tensor("wd", [NCH, 128, D], F32, kind="ExternalInput").ap()
    w0b_d = nc.dram_tensor("w0b", [128, D], F32, kind="ExternalInput").ap()
    w1b_d = nc.dram_tensor("w1b", [128, D], F32, kind="ExternalInput").ap()
    bb_d = nc.dram_tensor("bb", [128, D], F32, kind="ExternalInput").ap()
    dv_d = nc.dram_tensor("dv", [128, 2], F32, kind="ExternalInput").ap()
    cpk_d = nc.dram_tensor("cpk", [128, 8], F32, kind="ExternalInput").ap()
    eye_d = nc.dram_tensor("eye", [128, 128], BF16, kind="ExternalInput").ap()
    out_d = nc.dram_tensor("out", [R, D], F32, kind="ExternalOutput").ap()
    guc_d = nc.dram_tensor("guc", [NCH, 128, KT * 256], BF16, kind="Internal").ap()
    wdc_d = nc.dram_tensor("wdc", [NCH, 128, D], BF16, kind="Internal").ap()

    with contextlib.ExitStack() as st:
        e = st.enter_context
        A = nc.alloc_sbuf_tensor
        cpk = A("cpk_s", [128, 8], F32).ap()
        w0b = A("w0b_s", [128, D], BF16).ap()
        w1b = A("w1b_s", [128, D], BF16).ap()
        bb = A("bb_s", [128, D], BF16).ap()
        dvb = A("dvb", [128, 2], F32).ap()
        dvc = A("dvc", [128, 2], F32).ap()
        eye = A("eye_s", [128, 128], BF16).ap()
        xt = [A(f"xt{i}", [128, D], F32).ap() for i in range(2)]
        u_ = A("u", [128, D], BF16).ap()
        S0 = A("S0", [128, D], F32).ap()
        S1 = A("S1", [128, D], F32).ap()
        ytmp = A("ytmp", [128, D], F32).ap()
        vTT = [A(f"vTT{i}", [128, KT * G], BF16).ap() for i in range(2)]
        zb = A("zb", [128, NRT * D], F32).ap()
        hsp = A("hsp", [128, 4 * G], F32).ap()
        habs = A("habs", [128, 4 * G], BF16).ap()
        smt = A("smt", [128, 4 * G], BF16).ap()
        t2nd = A("t2nd", [128, G], BF16).ap()
        hs = A("hs", [128, WSP * 4 * G], BF16).ap()
        wpk = [A(f"wpk{i}", [128, KT * 256], BF16).ap() for i in range(3)]
        wdb = A("wdb", [128, WSP * 4 * D], BF16).ap()
        NSET = 2
        st6 = [A(f"st6_{i}", [128, 24], F32).ap() for i in range(NSET)]
        st2 = [A(f"st2_{i}", [128, 2], F32).ap() for i in range(NSET)]
        sdv = [A(f"sdv_{i}", [128, 1], F32).ap() for i in range(NSET)]
        rst = [A(f"rst_{i}", [128, 1], F32).ap() for i in range(NSET)]
        sg_t = A("sg", [128, G], F32).ap()
        pk = [A(f"pk{i}", [128, 8], F32).ap() for i in range(2)]
        vst = S1.bitcast(BF16)

        z_ps = nc.alloc_psum_tensor("z_ps", [128, D], F32).ap()
        gp_ps = [nc.alloc_psum_tensor(f"gp{i}", [128, G], F32).ap()
                 for i in range(2)]
        pp_ps = [nc.alloc_psum_tensor(f"pp{i}", [128, G], F32).ap()
                 for i in range(2)]
        zbT = z_ps.bitcast(BF16)
        trZ = [zbT[:, 0:128], zbT[:, 2048:2176]]
        trF = [gp_ps[0].bitcast(BF16)[:, 0:128], pp_ps[0].bitcast(BF16)[:, 0:128]]

        sxp = [Sem(nc, st, f"sxp{i}") for i in range(2)]
        sw = Sem(nc, st, "sw")
        sc = Sem(nc, st, "scst")
        sop = [Sem(nc, st, f"sop{i}") for i in range(NRT)]
        sv = Sem(nc, st, "sv")
        ss = Sem(nc, st, "ss")
        stt = Sem(nc, st, "stt")
        sgp = Sem(nc, st, "sgp")
        svt = Sem(nc, st, "svt")
        swbg = [Sem(nc, st, f"swbg{i}") for i in range(3)]
        swbw = Sem(nc, st, "swbw")
        swpg = [Sem(nc, st, f"swpg{i}") for i in range(3)]
        swpy = [Sem(nc, st, f"swpy{i}") for i in range(3)]
        sws = Sem(nc, st, "sws")
        scg = Sem(nc, st, "scg")

        streams = {k: [] for k in ("t", "v", "s", "g", "y")}

        def op(eng, builder, waits=(), incs=()):
            streams[eng].append((tuple(waits), builder, tuple(incs)))
            for sem, k in incs:
                sem.n += k

        op("y", lambda q: q.dma_start(out=cpk[:, :], in_=cpk_d[:, :]), incs=[(sc, 16)])
        op("y", lambda q: q.dma_start(out=dvb[:, :], in_=dv_d[:, :]), incs=[(sc, 16)])
        op("y", lambda q: q.dma_start(out=eye[:, :], in_=eye_d[:, :]), incs=[(sc, 16)])
        op("g", lambda q: q.dma_start(out=w0b[:, :], in_=w0b_d[:, :]), incs=[(scg, 16)])
        op("g", lambda q: q.dma_start(out=w1b[:, :], in_=w1b_d[:, :]), incs=[(scg, 16)])
        op("g", lambda q: q.dma_start(out=bb[:, :], in_=bb_d[:, :]), incs=[(scg, 16)])
        consts_ready = 16 * 3
        op("v", lambda q: q.tensor_scalar_max(dvc[:, :], dvb[:, :],
                                              cpk[:, C09:C09 + 1]),
           waits=[(sc, consts_ready)], incs=[(sv, 1)])
        op("v", lambda q: q.tensor_scalar_min(dvc[:, :], dvc[:, :],
                                              cpk[:, ONE:ONE + 1]),
           waits=[(sv, sv.n)], incs=[(sv, 1)])
        dv_ready = sv.n

        ln_eps = cpk[:, EPS:EPS + 1]
        one_c = cpk[:, ONE:ONE + 1]
        nh_c = cpk[:, NHALF:NHALF + 1]
        ph_c = cpk[:, PHALF:PHALF + 1]

        ln_cnt = [0]
        last_vapply = [0] * NSET
        last_ln = [0] * NSET

        def ln_stats(src_ap, extra_waits):
            si = ln_cnt[0] % NSET
            ln_cnt[0] += 1
            for cc in range(4):
                w = list(extra_waits) if cc == 0 else ()
                def b_(q, cc=cc, src=src_ap, si=si):
                    return q.bn_stats(st6[si][:, 6 * cc:6 * cc + 6],
                                      src[:, 512 * cc:512 * cc + 512])
                op("v", b_, waits=w, incs=[(sv, 1)] if cc == 3 else ())
            op("v", lambda q, si=si: q.bn_aggr(st2[si][:, 0:2], st6[si][:, 0:24]),
               waits=[(sv, sv.n), (ss, last_ln[si])], incs=[(sv, 1)])
            vv = sv.n
            op("s", lambda q, si=si: q.activation(sdv[si][:, 0:1], st2[si][:, 1:2],
                                                  AF.Ln, bias=ln_eps, scale=one_c),
               waits=[(sv, vv), (sc, consts_ready)], incs=[(ss, 1)])
            last_ln[si] = ss.n
            op("s", lambda q, si=si: q.activation(rst[si][:, 0:1], sdv[si][:, 0:1],
                                                  AF.Exp, scale=nh_c),
               waits=[(ss, ss.n), (sv, last_vapply[si])], incs=[(ss, 1)])
            return si, ss.n

        def apply_ln(dst_ap, src_ap, si, ssv, extra_waits=()):
            op("v", lambda q, si=si: q.tensor_scalar(
                dst_ap, src_ap, st2[si][:, 0:1], rst[si][:, 0:1],
                ALU.subtract, ALU.mult),
               waits=[(ss, ssv)] + list(extra_waits), incs=[(sv, 1)])
            last_vapply[si] = sv.n
            return sv.n

        x_done = {}
        y_free = {}
        out_sent = {}
        pe_chunk_done = {}
        pe_chunk_mid = {}
        down_done_w = {}
        silu_done_c = {}
        hmul_done_c = {}
        gu_ready = {}
        vt_evict = {}
        vtr_done = [0]
        burst_evict = {}
        last_par0 = [0, 0]
        last_out_add = [0]
        yrec_read = [0]
        last_vrec_tr = [0]
        last_span_mults = [0]
        last_c0h1 = [0]
        wb_done = {}
        wdwb_done = {}
        prev_evict = [0]
        H = D // 2

        def attn_steps(g1, rt, tr_slots, tr_guard_fn):
            gi = g1 * NRT + rt
            par = gi % 2
            row0 = g1 * G + rt * 128
            stv = {}

            def s1():
                w = [(sv, y_free[gi - 2])] if gi - 2 in y_free else []
                op("y", lambda q: q.dma_start(
                    out=xt[par][:, :], in_=x_d[row0:row0 + 128, :]),
                    waits=w, incs=[(sxp[par], 16)])
                x_done[gi] = (sxp[par], sxp[par].n)
                stv['ln1'] = ln_stats(xt[par], [x_done[gi]])

            def s2():
                si, ssv = stv['ln1']
                apply_ln(u_[:, :], xt[par][:, :], si, ssv)
                stv['uv'] = sv.n

            def s2b():
                op("v", lambda q: q.tensor_tensor(S0[:, 0:H], u_[:, 0:H],
                                                  w0b[:, 0:H], ALU.mult),
                   waits=[(sv, sv.n), (scg, 16 * 3)], incs=[(sv, 1)])
                stv['c0h0'] = sv.n
                op("v", lambda q: q.tensor_tensor(S0[:, H:D], u_[:, H:D],
                                                  w0b[:, H:D], ALU.mult),
                   waits=[(sv, sv.n)], incs=[(sv, 1)])
                stv['c0h1'] = sv.n

            def s3():
                dvf = dvc[:, 1:2].broadcast_to([128, D])
                op("v", lambda q, dvf=dvf: q.tensor_tensor_scan(
                    S1[:, :], dvf, u_[:, :], 0.0, ALU.mult, ALU.add),
                    waits=[(sv, sv.n), (svt, svt.n)], incs=[(sv, 1)])
                stv['s1'] = sv.n

            def s3b():
                op("v", lambda q: q.tensor_tensor(S1[:, :], S1[:, :], w1b[:, :],
                                                  ALU.mult),
                   waits=[(sv, sv.n), (scg, 16 * 3)], incs=[(sv, 1)])
                op("v", lambda q: q.tensor_tensor(S1[:, :], S1[:, :], bb[:, :],
                                                  ALU.add),
                   waits=[(sv, sv.n)], incs=[(sv, 1)])
                stv['t3'] = sv.n

            def s4():
                dv0 = dvc[:, 0:1].broadcast_to([128, H])
                op("v", lambda q, dv0=dv0: q.tensor_tensor_scan(
                    S0[:, 0:H], dv0, S0[:, 0:H], 0.0, ALU.mult, ALU.add),
                    waits=[(sv, stv['c0h0'])], incs=[(sv, 1)])
                op("v", lambda q, dv0=dv0: q.tensor_tensor_scan(
                    S0[:, H:D], dv0, S0[:, H:D], S0[:, H - 1:H], ALU.mult,
                    ALU.add),
                    waits=[(sv, sv.n)], incs=[(sv, 1)])

            def s5():
                op("v", lambda q: q.tensor_tensor(S0[:, :], S0[:, :], S1[:, :],
                                                  ALU.add),
                   waits=[(sv, sv.n)], incs=[(sv, 1)])
                stv['ln2'] = ln_stats(S0, [(sv, sv.n)])

            def s6():
                si, ssv = stv['ln2']
                apply_ln(S1[:, :], S0[:, :], si, ssv)
                op("v", lambda q: q.tensor_tensor(
                    ytmp[:, :], xt[par][:, :], S1[:, :], ALU.add),
                    waits=[(sv, sv.n)], incs=[(sv, 1)])
                y_free[gi] = sv.n
                stv['ln3'] = ln_stats(ytmp, [(sv, sv.n)])

            def s7():
                si, ssv = stv['ln3']
                gpar = g1 % 2
                op("s", lambda q, si=si: q.copy(pk[gpar][:, rt * 2:rt * 2 + 1],
                                                st2[si][:, 0:1]),
                   incs=[(ss, 1)])
                op("s", lambda q, si=si: q.activation(
                    pk[gpar][:, rt * 2 + 1:rt * 2 + 2], sdv[si][:, 0:1],
                    AF.Exp, scale=ph_c),
                   incs=[(ss, 1)])
                last_ln[si] = ss.n
                stv['v'] = apply_ln(vst[:, 0:D], ytmp[:, :], si, ssv)

            def s8():
                dstv = vTT[g1 % 2]
                for k in range(KT):
                    w = []
                    if k == 0:
                        w = [(sv, stv['v']), (stt, last_vrec_tr[0])]
                    op("s", lambda q, k=k, dstv=dstv: q.dma_start(
                        out=dstv[:, k * G + rt * 128:k * G + rt * 128 + 128],
                        in_=vst[:, k * 128:(k + 1) * 128], transpose=True),
                        waits=w, incs=[(svt, 16)])
                vt_evict[(g1, rt)] = svt.n

            return [s1, s2, s2b, s3, s3b, s4, s5, s6, s7, s8]

        def emit_gu(g1, cix_lo, n):
            for cc in range(cix_lo, min(cix_lo + n, NCH)):
                qix = g1 * NCH + cc
                par = qix % 3
                HB = KT * 128
                for h in range(2):
                    w = []
                    if qix >= 3:
                        w.append((stt, pe_chunk_mid[qix - 3] if h == 0
                                  else pe_chunk_done[qix - 3]))
                        if (qix - 3) in wb_done:
                            w.append(wb_done[qix - 3])
                        if g1 >= 1:
                            w.append(wb_done[cc])
                    lo, hi = h * HB, (h + 1) * HB
                    if g1 == 0:
                        op("g", lambda q, cc=cc, par=par, lo=lo, hi=hi:
                           q.dma_start(out=wpk[par][:, lo:hi],
                                       in_=gu_d[cc, :, lo:hi]),
                           waits=w, incs=[(swpg[par], 16)])
                    else:
                        op("y", lambda q, cc=cc, par=par, lo=lo, hi=hi:
                           q.dma_start(out=wpk[par][:, lo:hi],
                                       in_=guc_d[cc, :, lo:hi]),
                           waits=w, incs=[(swpy[par], 16)])
                gu_ready[qix] = (swpg[par], swpg[par].n) if g1 == 0 \
                    else (swpy[par], swpy[par].n)
                if g1 == 0:
                    op("y", lambda q, cc=cc, par=par: q.dma_start(
                        out=guc_d[cc, :, :], in_=wpk[par][:, :]),
                       waits=[gu_ready[qix]], incs=[(swbg[par], 16)])
                    wb_done[qix] = (swbg[par], swbg[par].n)

        def ffn_group(g1, pump, enqueue, run_tr):
            vT_done = max(vt_evict[(g1, rt)] for rt in range(NRT))
            emit_gu(g1, 0, 3)
            for wdx in range(NW):
                enqueue(g1, wdx)
                hs_ready = {}
                for s2 in range(WSP):
                    w = []
                    if s2 == 0:
                        if (g1, wdx - 1) in down_done_w:
                            w.append((stt, down_done_w[(g1, wdx - 1)]))
                        elif wdx == 0 and (g1 - 1, NW - 1) in down_done_w:
                            w.append((stt, down_done_w[(g1 - 1, NW - 1)]))
                        if (g1, wdx - 1) in wdwb_done:
                            w.append(wdwb_done[(g1, wdx - 1)])
                        elif wdx == 0 and (g1 - 1, NW - 1) in wdwb_done:
                            w.append(wdwb_done[(g1 - 1, NW - 1)])
                    c0w = (wdx * WSP + s2) * 4
                    dst = wdb[:, s2 * 4 * D:(s2 + 1) * 4 * D] \
                        .rearrange("p (r h) -> p r h", h=D)
                    if g1 == 0:
                        op("g", lambda q, c0w=c0w, dst=dst: q.dma_start(
                            out=dst, in_=wd_d[c0w:c0w + 4, :, :]
                            .rearrange("r p h -> p r h")),
                            waits=w, incs=[(sw, 16)])
                    else:
                        op("s", lambda q, c0w=c0w, dst=dst: q.dma_start(
                            out=dst, in_=wdc_d[c0w:c0w + 4, :, :]
                            .rearrange("r p h -> p r h")),
                            waits=w, incs=[(sws, 16)])
                wd_ready = (sw, sw.n) if g1 == 0 else (sws, sws.n)
                if g1 == 0:
                    c0w = wdx * WSP * 4
                    op("y", lambda q, c0w=c0w: q.dma_start(
                        out=wdc_d[c0w:c0w + WSP * 4, :, :]
                        .rearrange("r p h -> p r h"),
                        in_=wdb[:, :].rearrange("p (r h) -> p r h", h=D)),
                       waits=[wd_ready], incs=[(swbw, 16)])
                    wdwb_done[(g1, wdx)] = (swbw, swbw.n)
                for s_ in range(WSP):
                    sp = wdx * WSP + s_
                    for r in range(4):
                        cix = sp * 4 + r
                        qix = g1 * NCH + cix
                        par = qix % 3
                        gps, pps = gp_ps[qix % 2], pp_ps[qix % 2]
                        if r == 2 and cix >= 2:
                            emit_gu(g1, cix + 1, 2)
                        w0_ = [gu_ready[qix], (svt, vT_done)]
                        if qix - 2 in silu_done_c:
                            w0_.append((ss, silu_done_c[qix - 2]))
                        if qix - 2 in hmul_done_c:
                            w0_.append((sv, hmul_done_c[qix - 2]))
                        for k in range(KT):
                            op("t", lambda q, par=par, k=k, gps=gps: q.matmul(
                                gps[:, :], wpk[par][:, 256 * k:256 * k + 128],
                                vTT[g1 % 2][:, k * G:(k + 1) * G],
                                start=(k == 0), stop=(k == KT - 1),
                                skip_group_check=True),
                                waits=w0_ if k == 0 else ())
                            op("t", lambda q, par=par, k=k, pps=pps: q.matmul(
                                pps[:, :],
                                wpk[par][:, 256 * k + 128:256 * (k + 1)],
                                vTT[g1 % 2][:, k * G:(k + 1) * G],
                                start=(k == 0), stop=(k == KT - 1),
                                skip_group_check=True),
                                incs=[(stt, 1)]
                                if k in (KT // 2 - 1, KT - 1) else ())
                            if k == KT // 2 - 1:
                                pe_chunk_mid[qix] = stt.n
                        gp_done = stt.n
                        pe_chunk_done[qix] = gp_done
                        sw_ = [(stt, gp_done)]
                        if qix - 1 in hmul_done_c:
                            sw_.append((sv, hmul_done_c[qix - 1]))
                        op("s", lambda q, gps=gps: q.activation(
                            sg_t[:, :], gps[:, :], AF.Silu),
                           waits=sw_, incs=[(ss, 1)])
                        silu_done_c[qix] = ss.n
                        if par == 0:
                            last_par0[0] = ss.n
                        hw = [(ss, silu_done_c[qix])]
                        if r == 0 and last_span_mults[0]:
                            hw.append((sgp, last_span_mults[0]))
                        if qix == g1 * NCH and last_out_add[0]:
                            hw.append((sv, last_out_add[0]))
                        op("v", lambda q, r=r, pps=pps: q.tensor_tensor(
                            hsp[:, r * G:(r + 1) * G], sg_t[:, :], pps[:, :],
                            ALU.mult),
                            waits=hw, incs=[(sv, 1)])
                        hmul_done_c[qix] = sv.n
                        if par == 0:
                            last_par0[1] = sv.n
                        aw = [(sv, hmul_done_c[qix])]
                        if r == 0 and last_span_mults[0]:
                            aw.append((sgp, last_span_mults[0]))
                        if qix == g1 * NCH and last_out_add[0]:
                            aw.append((sv, last_out_add[0]))
                        op("s", lambda q, r=r: q.activation(
                            habs[:, r * G:(r + 1) * G], hsp[:, r * G:(r + 1) * G],
                            AF.Abs),
                           waits=aw, incs=[(ss, 1)])
                        if (sp % WSP) * 4 + r < 6:
                            pump(1)
                    habs_done = ss.n
                    if sp + 1 < NSP:
                        emit_gu(g1, 4 * (sp + 1) + 1, 2)
                    hv = [habs[:, r * G:(r + 1) * G] for r in range(4)]
                    mv = [smt[:, i * G:(i + 1) * G] for i in range(4)]
                    op("v", lambda q, hv=hv, mv=mv: q.tensor_tensor(
                        mv[0], hv[0], hv[1], ALU.max), waits=[(ss, habs_done)])
                    op("v", lambda q, hv=hv, mv=mv: q.tensor_tensor(
                        mv[1], hv[0], hv[1], ALU.min))
                    op("v", lambda q, hv=hv, mv=mv: q.tensor_tensor(
                        mv[2], hv[2], hv[3], ALU.max))
                    op("v", lambda q, hv=hv, mv=mv: q.tensor_tensor(
                        mv[3], hv[2], hv[3], ALU.min), incs=[(sv, 1)])
                    op("v", lambda q, mv=mv: q.tensor_tensor(
                        mv[0], mv[0], mv[2], ALU.min), waits=[(sv, sv.n)])
                    op("v", lambda q, mv=mv: q.tensor_tensor(
                        mv[1], mv[1], mv[3], ALU.max), incs=[(sv, 1)])
                    op("v", lambda q, mv=mv: q.tensor_tensor(
                        t2nd[:, :], mv[0], mv[1], ALU.max),
                       waits=[(sv, sv.n)], incs=[(sv, 1)])
                    for r in range(4):
                        op("v", lambda q, r=r: q.tensor_tensor(
                            habs[:, r * G:(r + 1) * G], habs[:, r * G:(r + 1) * G],
                            t2nd[:, :], ALU.is_ge),
                           waits=[(sv, sv.n)] if r == 0 else (),
                           incs=[(sv, 1)] if r == 3 else ())
                    ge_done = sv.n
                    for r in range(4):
                        slot = s_ * 4 + r
                        w = [(sv, ge_done)] if r == 0 else []
                        if s_ == 0 and r == 0:
                            if (g1, wdx - 1) in down_done_w:
                                w.append((stt, down_done_w[(g1, wdx - 1)]))
                            elif wdx == 0 and (g1 - 1, NW - 1) in down_done_w:
                                w.append((stt, down_done_w[(g1 - 1, NW - 1)]))
                        op("g", lambda q, r=r, slot=slot: q.tensor_tensor(
                            hs[:, slot * G:(slot + 1) * G],
                            hsp[:, r * G:(r + 1) * G],
                            habs[:, r * G:(r + 1) * G], ALU.mult),
                           waits=w, incs=[(sgp, 1)] if r == 3 else ())
                    hs_ready[s_] = sgp.n
                    last_span_mults[0] = sgp.n
                for rt in range(NRT):
                    nt = WSP * 4
                    for tix in range(nt):
                        s_, r = divmod(tix, 4)
                        w = []
                        if r == 0 and rt == 0:
                            w.append((sgp, hs_ready[s_]))
                        if tix == 0:
                            w.append(wd_ready)
                            if prev_evict[0]:
                                w.append((sv, prev_evict[0]))
                        for qq in range(4):
                            op("t", lambda q, tix=tix, rt=rt, qq=qq: q.matmul(
                                z_ps[:, qq * 512:(qq + 1) * 512],
                                hs[:, tix * G + rt * 128:tix * G + rt * 128 + 128],
                                wdb[:, tix * D + qq * 512:tix * D + (qq + 1) * 512],
                                start=(tix == 0), stop=(tix == nt - 1),
                                skip_group_check=True),
                                waits=w if qq == 0 else (),
                                incs=[(stt, 1)]
                                if (qq == 3 and tix == nt - 1) else ())
                        w = []
                    ddone = stt.n
                    if rt == NRT - 1:
                        down_done_w[(g1, wdx)] = ddone
                    if wdx == 0:
                        wz = [(stt, ddone)]
                        if (g1 - 1, rt) in out_sent:
                            wz.append(out_sent[(g1 - 1, rt)])
                        op("v", lambda q, rt=rt: q.tensor_copy(
                            zb[:, rt * D:(rt + 1) * D], z_ps[:, :]),
                           waits=wz, incs=[(sv, 1)])
                    else:
                        op("v", lambda q, rt=rt: q.tensor_tensor(
                            zb[:, rt * D:(rt + 1) * D], zb[:, rt * D:(rt + 1) * D],
                            z_ps[:, :], ALU.add),
                           waits=[(stt, ddone)], incs=[(sv, 1)])
                    prev_evict[0] = sv.n

        def ln4_group(g1, pump):
            z_done = prev_evict[0]
            gpar = g1 % 2
            for rt in range(NRT):
                ev = {}
                for k in range(KT):
                    ps = k % 2
                    w = []
                    if k < 2:
                        w.append((sv, prev_evict[0]))
                        w.append((sgp, sgp.n))
                        if yrec_read[0]:
                            w.append((sv, yrec_read[0]))
                    else:
                        w.append((ss, ev[k - 2]))
                    op("t", lambda q, k=k, ps=ps, g1=g1, rt=rt: q.transpose(
                        trZ[ps], vTT[g1 % 2][:, k * G + rt * 128:
                                             k * G + rt * 128 + 128], eye[:, :]),
                        waits=w, incs=[(stt, 1)])
                    vtr_done[0] = stt.n
                    op("s", lambda q, k=k, ps=ps: q.copy(
                        habs[:, k * 128:(k + 1) * 128], trZ[ps]),
                        waits=[(stt, stt.n)], incs=[(ss, 1)])
                    ev[k] = ss.n
                vrec_done = ss.n
                last_vrec_tr[0] = vtr_done[0]
                si, ssv = ln_stats(zb[:, rt * D:(rt + 1) * D],
                                   [(sv, z_done)] if rt == 0 else [])
                apply_ln(hsp[:, 0:D], zb[:, rt * D:(rt + 1) * D], si, ssv,
                         extra_waits=[(sgp, sgp.n)] if rt == 0 else ())
                op("v", lambda q, rt=rt: q.tensor_scalar(
                    ytmp[:, :], habs[:, 0:D],
                    pk[gpar][:, rt * 2 + 1:rt * 2 + 2],
                    pk[gpar][:, rt * 2:rt * 2 + 1],
                    ALU.mult, ALU.add),
                   waits=[(ss, vrec_done)], incs=[(sv, 1)])
                yrec_read[0] = sv.n
                op("v", lambda q, rt=rt: q.tensor_tensor(
                    zb[:, rt * D:(rt + 1) * D], ytmp[:, :], hsp[:, 0:D],
                    ALU.add),
                   waits=[(sv, sv.n)], incs=[(sv, 1)])
                last_out_add[0] = sv.n
                row0 = g1 * G + rt * 128
                op("s", lambda q, rt=rt, row0=row0: q.dma_start(
                    out=out_d[row0:row0 + 128, :],
                    in_=zb[:, rt * D:(rt + 1) * D]),
                    waits=[(sv, sv.n)], incs=[(sop[rt], 16)])
                out_sent[(g1, rt)] = (sop[rt], sop[rt].n)
                pump(2)

        pending = []
        pend_tr = {}

        def pump(n):
            cnt = 0
            while pending and cnt < n:
                pending.pop(0)()
                cnt += 1

        def enqueue(g1, wdx):
            if g1 + 1 >= NG or wdx % 2 != 0:
                return
            rt = wdx // 2
            sts = attn_steps(g1 + 1, rt, None, None)
            pending.extend(sts[:IL_K])
            tail_steps.append(sts[IL_K:])

        def run_tr(g1, wdx):
            pass

        def no_pump(n):
            pass

        for rt in range(NRT):
            for fn in attn_steps(0, rt, None, None):
                fn()

        def no_enqueue(g1, wdx):
            pass

        def no_tr(g1, wdx):
            pass

        IL_K = 10
        tail_steps = []
        for g in range(NG):
            if g + 1 < NG:
                ffn_group(g, pump, enqueue, run_tr)
            else:
                ffn_group(g, pump, no_enqueue, no_tr)
            pump(99)
            for tail in tail_steps:
                for fn in tail:
                    fn()
            tail_steps = []
            ln4_group(g, no_pump)

        fin = [(s, s.n) for s in sop]
        op("y", lambda q: [q.wait_ge(s.h, v) for s, v in fin])

        block = e(nc.Block())

        def runner(stream):
            def run(q):
                for waits, builder, incs in stream:
                    for sem, val in waits:
                        q.wait_ge(sem.h, val)
                    ins = builder(q)
                    for sem, k in incs:
                        ins.then_inc(sem.h, k)
            return run

        block.tensor(runner(streams["t"]))
        block.vector(runner(streams["v"]))
        block.scalar(runner(streams["s"]))
        block.gpsimd(runner(streams["g"]))
        block.sync(runner(streams["y"]))
    return nc


def kernel(L0_attn_x, L0_attn_weight, L0_attn_bias, L0_attn_decay_value,
           L0_ffn_w_gate, L0_ffn_w_up, L0_ffn_w_down):
    x = np.ascontiguousarray(np.asarray(L0_attn_x, dtype=np.float32)
                             .reshape(NCORES * R, D))
    wg = np.asarray(L0_ffn_w_gate, np.float32).reshape(D, NSP, 128, 4)
    wg = wg.transpose(1, 3, 0, 2).reshape(NCH, D, 128)
    wu = np.asarray(L0_ffn_w_up, np.float32).reshape(D, NSP, 128, 4)
    wu = wu.transpose(1, 3, 0, 2).reshape(NCH, D, 128)
    wdn = np.asarray(L0_ffn_w_down, np.float32).reshape(NSP, 128, 4, D)
    wdn = wdn.transpose(0, 2, 1, 3).reshape(NCH, 128, D)
    wgt = wg.reshape(NCH, KT, 128, 128).transpose(0, 2, 1, 3)
    wut = wu.reshape(NCH, KT, 128, 128).transpose(0, 2, 1, 3)
    gu = np.stack([wgt, wut], axis=3).reshape(NCH, 128, KT * 256)
    gu = np.ascontiguousarray(gu)
    wdn = np.ascontiguousarray(wdn)

    w = np.asarray(L0_attn_weight, np.float32)
    w0b = np.ascontiguousarray(np.broadcast_to(w[0:1, :], (128, D)))
    w1b = np.ascontiguousarray(np.broadcast_to(w[1:2, :], (128, D)))
    bbh = np.ascontiguousarray(np.broadcast_to(
        np.asarray(L0_attn_bias, np.float32)[None, :], (128, D)))
    dv = np.ascontiguousarray(np.broadcast_to(
        np.asarray(L0_attn_decay_value, np.float32).reshape(1, 2), (128, 2)))
    cpk = np.zeros((128, 8), np.float32)
    cpk[:, EPS] = 1e-5
    cpk[:, ONE] = 1.0
    cpk[:, C09] = 0.9
    cpk[:, CM1] = -1.0
    cpk[:, NHALF] = -0.5
    cpk[:, PHALF] = 0.5
    eye = np.eye(128, dtype=ml_dtypes.bfloat16)

    nc = build_core()
    in_maps = []
    for c in range(NCORES):
        in_maps.append({
            "x": x[c * R:(c + 1) * R], "gu": gu, "wd": wdn, "w0b": w0b,
            "w1b": w1b, "bb": bbh, "dv": dv, "cpk": cpk, "eye": eye,
        })
    res = run_bass_kernel_spmd(nc, in_maps, core_ids=list(range(NCORES)))
    out = np.concatenate([r["out"] for r in res.results], axis=0)
    return out.reshape(4, 4096, D)
